# version 03
# baseline (speedup 1.0000x reference)
import os
import numpy as np
import concourse.bass as bass
import concourse.mybir as mybir
from concourse.bass_utils import run_bass_kernel_spmd
from contextlib import ExitStack

F32 = mybir.dt.float32
BF16 = mybir.dt.bfloat16
F16 = mybir.dt.float16
AF = mybir.ActivationFunctionType
OP = mybir.AluOpType

D, DI, NS, DTR = 1024, 2048, 16, 64
NB, T, NCH, TAU = 2, 2048, 4, 512
NE, HF, DP = 4, 2048, 512
SEG = TAU + 1
QSEG = 4
QLEN = QSEG * SEG
EPS = 1e-5
NIB = NCH * NB
MK_STOP = os.environ.get("MK_STOP", "")
MK_DBG = os.environ.get("MK_DBG", "")


def ts(i, n):
    return slice(i * n, (i + 1) * n)


class BD:

    def __init__(self, nc, block, es):
        self.nc, self.block = nc, block
        self.sem = {ch: es.enter_context(nc.semaphore("sem_" + ch))
                    for ch in ("v", "s", "t", "g", "dy", "dg")}
        self.cnt = {ch: 0 for ch in self.sem}
        self.waited = {}
        self.secs = {"v": block.vector, "s": block.scalar, "t": block.tensor,
                     "g": block.gpsimd, "y": block.sync}
        self.barrier = []

    def bar(self):
        self.barrier = [(ch, self.cnt[ch]) for ch in self.sem]

    def _wait(self, p, eng, deps):
        need = {}
        for (ch, val) in list(deps) + self.barrier:
            if ch in ("dy", "dg") and val > 0:
                val = self.cnt[ch]
            if val > 0 and need.get(ch, 0) < val:
                need[ch] = val
        for ch, val in sorted(need.items()):
            key = (eng, ch)
            if self.waited.get(key, 0) >= val:
                continue
            p.wait_ge(self.sem[ch], val)
            self.waited[key] = val

    def emit(self, eng, fn, deps=()):
        out = {}

        def body(p):
            self._wait(p, eng, list(deps) + [(eng, self.cnt[eng])])
            fn(p).then_inc(self.sem[eng], 1)
            self.cnt[eng] += 1
            out["t"] = (eng, self.cnt[eng])
        self.secs[eng](body)
        return out["t"]

    def dma(self, issuer, fn, deps=()):
        ch = "dy" if issuer == "y" else "dg"
        out = {}

        def body(p):
            self._wait(p, issuer, deps)
            fn(p).then_inc(self.sem[ch], 16)
            self.cnt[ch] += 16
            out["t"] = (ch, self.cnt[ch])
        self.secs[issuer](body)
        return out["t"]

    def raw(self, eng, fn):
        self.secs[eng](fn)


def build_nc():
    nc = bass.Bass("TRN2", target_bir_lowering=False, dynamic_dma_scratch_size=8192)
    dram = {}

    def din(name, shape):
        dram[name] = nc.dram_tensor(name, list(shape), F32, kind="ExternalInput")
        return dram[name]

    din("x", (NB, T, D))
    din("w_in", (32, 8, 128, 128))
    din("w_x", (128, 1536))
    din("w_dt", (64, 2048))
    din("w_out", (8, 16, 128, 128))
    din("ew1", (NE, 16, 8, 128, 128))
    din("ew2", (NE, 8, 16, 128, 128))
    din("wmod", (48, 8, 128, 128))
    din("dp_w1", (4, 8, 128, 128))
    din("dp_w2", (128, 4))
    din("gate_w", (128, 32))
    din("a_log", (128, 16 * NS))
    din("colpack", (128, 208))
    din("thrs", (4, 2))
    din("ltm", (16, 1))
    din("pcmp", (4, 16))
    din("prnk", (16, 4))
    din("ones4", (4, 4))
    din("cT", (128, 8 * NB))
    din("ident", (128, 128))
    din("onesd", (128, 128))
    din("dpb2", (1, 1))
    out_t = nc.dram_tensor("out", [NB, T, D], F32, kind="ExternalOutput")
    dbg_t = nc.dram_tensor("dbg", [128, 8208], F32, kind="ExternalOutput")

    sc = {}
    sc["w_in"] = nc.dram_tensor("s_w_in", [32, 8, 128, 128], BF16)
    sc["w_out"] = nc.dram_tensor("s_w_out", [8, 16, 128, 128], BF16)
    sc["ew1"] = nc.dram_tensor("s_ew1", [NE, 16, 8, 128, 128], BF16)
    sc["ew2"] = nc.dram_tensor("s_ew2", [NE, 8, 16, 128, 128], BF16)
    sc["dp_w1"] = nc.dram_tensor("s_dp_w1", [4, 8, 128, 128], BF16)

    C_BIN, C_BDT, C_D, C_LNG, C_LNB, C_BOUT, C_EB1, C_EB2, C_DPB1 = (
        0, 32, 48, 64, 80, 96, 104, 168, 200)

    es = ExitStack()

    def sb(name, p, f, dt):
        return es.enter_context(nc.sbuf_tensor(name, [p, f], dt))

    stg_i = sb("stg_i", 128, 1024, F32)
    stg_o = sb("stg_o", 128, 1024, F32)
    xres = sb("xres", 128, 8 * 512, F32)
    h1 = sb("h1", 128, 8 * 512, BF16)
    xin = sb("xin", 128, 16 * 512, BF16)
    zs = sb("zs", 128, 16 * 512, BF16)
    dtb = sb("dtb", 128, 16 * 512, BF16)
    yac = sb("yac", 128, 16 * 512, BF16)
    abar = sb("abar", 128, 2 * QLEN, F16)
    rstp = sb("rstp", 128, 2 * QLEN, BF16)
    hstp = sb("hstp", 128, 2 * QLEN, BF16)
    brep = sb("brep", 128, 8 * 512, BF16)
    crep = sb("crep", 128, 8 * 512, BF16)
    stat = sb("stat", 128, NB * 16 * NS, F32)
    h2 = sb("h2", 128, 8 * 512, BF16)
    hid = sb("hid", 128, 16 * 512, BF16)
    sts = sb("sts", 128, 3 * 512, F32)
    gms = sb("gms", 128, 2 * 512, BF16)
    lnc = sb("lnc", 128, 2 * 512, BF16)
    gsc = sb("gsc", 128, 2 * 512, F32)
    gbb = sb("gbb", 128, 2 * 512, BF16)
    wbc = sb("wbc", 128, 4 * 512, BF16)
    wrw = sb("wrw", 4, 512, BF16)
    dtr_s = sb("dtr_s", 64, 512, BF16)
    bcs = sb("bcs", 32, 512, BF16)
    wxs = sb("wxs", 128, 16 * 96, BF16)
    wdts = sb("wdts", 64, 16 * 128, BF16)
    dpw2s = sb("dpw2s", 128, 4, BF16)
    gws = sb("gws", 128, 8 * 4, BF16)
    a_sb = sb("a_sb", 128, 16 * NS, F32)
    idn = sb("idn", 128, 128, F32)
    onds = sb("onds", 128, 128, BF16)
    on4s = sb("on4s", 4, 4, BF16)
    cols = sb("cols", 128, 208, F32)
    thr_s = sb("thr_s", 4, 2, F32)
    ltms = sb("ltms", 16, 1, F32)
    pcmps = sb("pcmps", 4, 16, BF16)
    prnks = sb("prnks", 16, 4, BF16)
    mods = sb("mods", 128, 96, F32)
    modp = sb("modp", 128, 112, F32)
    cTs = sb("cTs", 128, 8 * NB, F32)
    dpb2s = sb("dpb2s", 1, 1, F32)
    wi_r = sb("wi_r", 128, 2 * 1024, BF16)
    wo_r = sb("wo_r", 128, 2 * 2048, BF16)
    e1_r = sb("e1_r", 128, 2 * 1024, BF16)

    psA = es.enter_context(nc.psum_tensor("psA", [128, 2048], F32))
    psB = es.enter_context(nc.psum_tensor("psB", [128, 2048], F32))

    def col(base, i):
        return cols[:, base + i: base + i + 1]

    with es, nc.Block() as block:
        bd = BD(nc, block, es)

        def modcol(slot, b, dmt):
            c = slot * 16 + dmt * 2 + b
            return modp[:, c:c + 1]

        for name, dst in (("ident", idn), ("colpack", cols), ("thrs", thr_s),
                          ("ltm", ltms), ("dpb2", dpb2s), ("a_log", a_sb)):
            bd.dma("y", lambda p, n=name, d=dst: p.dma_start(out=d[:], in_=dram[n].ap()))
        bd.dma("y", lambda p: p.dma_start(out=mods[:, 0:16], in_=dram["cT"].ap()))
        for name, dst in (("onesd", onds), ("ones4", on4s),
                          ("pcmp", pcmps), ("prnk", prnks), ("w_x", wxs),
                          ("w_dt", wdts), ("dp_w2", dpw2s), ("gate_w", gws)):
            t = dram[name]
            n_el = int(np.prod(t.shape))
            pf = n_el // dst.shape[0]
            bd.dma("g", lambda p, t=t, d=dst, pf=pf: p.dma_start(
                out=d[:, 0:pf], in_=bass.AP(t, 0, [[pf, d.shape[0]], [1, pf]])))
        bd.bar()

        bd.emit("s", lambda p: p.activation(out=a_sb[:], in_=a_sb[:], func=AF.Exp))
        bd.emit("v", lambda p: p.tensor_scalar_mul(a_sb[:], a_sb[:], -1.0),
                deps=[("s", bd.cnt["s"])])
        tcs = bd.emit("s", lambda p: p.activation(out=cTs[:], in_=mods[:, 0:8 * NB], func=AF.Silu))
        bd.emit("v", lambda p: p.memset(stat[:], 0.0))
        bd.emit("v", lambda p: p.memset(abar[:], 0.0))
        bd.emit("v", lambda p: p.memset(rstp[:], 0.0))
        bd.emit("v", lambda p: p.memset(hstp[:], 0.0))
        bd.bar()

        prev_mm = {}
        wmf = (stg_i, stg_o)
        for mt in range(48):
            buf = wmf[mt % 2][:, 0:1024]
            deps = [prev_mm[mt - 2]] if mt >= 2 else []
            tw = bd.dma("y", lambda p, mt=mt, buf=buf: p.dma_start(
                out=buf[:, 0:1024],
                in_=bass.AP(dram["wmod"], mt * 8 * 128 * 128,
                            [[128, 128], [128 * 128, 8], [1, 128]])), deps=deps)
            for dc in range(8):
                tm = bd.emit("t", lambda p, mt=mt, dc=dc, buf=buf: p.matmul(
                    psB[:, 2 * mt: 2 * mt + 2], buf[:, ts(dc, 128)],
                    cTs[:, 2 * dc: 2 * dc + 2], start=(dc == 0), stop=(dc == 7)),
                    deps=[tw, tcs])
            prev_mm[mt] = tm
        bd.bar()
        bd.emit("s", lambda p: p.activation(out=mods[:], in_=psB[:, 0:96], func=AF.Identity))
        bd.bar()
        bd.emit("v", lambda p: p.tensor_copy(modp[:, 0:16], mods[:, 0:16]))
        bd.emit("v", lambda p: p.tensor_scalar_add(modp[:, 16:32], mods[:, 16:32], 1.0))
        bd.emit("v", lambda p: p.tensor_copy(modp[:, 48:64], mods[:, 48:64]))
        bd.emit("v", lambda p: p.tensor_scalar_add(modp[:, 64:80], mods[:, 64:80], 1.0))
        for src, dst in ((32, 32), (80, 80)):
            bd.emit("v", lambda p, src=src: p.tensor_scalar(
                out=sts[:, 0:16], in0=mods[:, src:src + 16], scalar1=0.0,
                scalar2=2.0, op0=OP.is_ge, op1=OP.mult))
            bd.emit("v", lambda p: p.tensor_scalar_add(sts[:, 0:16], sts[:, 0:16], -1.0),
                    deps=[("v", bd.cnt["v"])])
            bd.emit("v", lambda p, src=src: p.tensor_tensor(
                out=sts[:, 16:32], in0=mods[:, src:src + 16], in1=sts[:, 0:16],
                op=OP.mult), deps=[("v", bd.cnt["v"])])
            bd.emit("v", lambda p: p.tensor_scalar_max(sts[:, 16:32], sts[:, 16:32], 0.25),
                    deps=[("v", bd.cnt["v"])])
            bd.emit("v", lambda p, dst=dst: p.tensor_tensor(
                out=modp[:, dst:dst + 16], in0=sts[:, 16:32], in1=sts[:, 0:16],
                op=OP.mult), deps=[("v", bd.cnt["v"])])
        for b in range(NB):
            for dmt in range(8):
                bd.emit("v", lambda p, b=b, dmt=dmt: p.tensor_tensor(
                    out=modcol(6, b, dmt), in0=modcol(2, b, dmt),
                    in1=col(C_BOUT, dmt), op=OP.mult))
        bd.bar()

        def cache(name, n_chunks, chunk_elems):
            t = dram[name]
            st = sc[name]
            pf = chunk_elems // 128
            prev = {}
            for i in range(n_chunks):
                slot = wo_r[:, ts(i % 2, 2048)][:, 0:pf]
                deps = [prev[i - 2]] if i >= 2 else []
                tl = bd.dma("g", lambda p, t=t, i=i, slot=slot, pf=pf: p.dma_start(
                    out=slot, in_=bass.AP(t, i * pf * 128, [[pf, 128], [1, pf]])),
                    deps=deps)
                tst = bd.dma("y", lambda p, st=st, i=i, slot=slot, pf=pf: p.dma_start(
                    out=bass.AP(st, i * pf * 128, [[pf, 128], [1, pf]]), in_=slot),
                    deps=[tl])
                prev[i] = tst

        cache("w_in", 32, 8 * 128 * 128)
        cache("w_out", 8, 16 * 128 * 128)
        cache("ew1", 64, 8 * 128 * 128)
        cache("ew2", 32, 16 * 128 * 128)
        cache("dp_w1", 4, 8 * 128 * 128)
        bd.bar()

        def ln_feature_major(src_ap_of, dst_of, nchunks, ones_t, mod_apply, ib_tag, iscale=1.0 / D):
            for i in range(nchunks):
                tc1 = bd.emit("s", lambda p, i=i: p.activation(
                    out=lnc[:, 0:512], in_=src_ap_of(i), func=AF.Identity),
                    deps=[("t", bd.cnt["t"])])
                tc2 = bd.emit("s", lambda p, i=i: p.activation(
                    out=lnc[:, 512:1024], in_=src_ap_of(i), func=AF.Square),
                    deps=[tc1])
                bd.emit("t", lambda p, i=i: p.matmul(
                    psB[:, 0:512], ones_t[:], lnc[:, 0:512],
                    start=(i == 0), stop=(i == nchunks - 1)), deps=[tc1])
                bd.emit("t", lambda p, i=i: p.matmul(
                    psB[:, 512:1024], ones_t[:], lnc[:, 512:1024],
                    start=(i == 0), stop=(i == nchunks - 1)), deps=[tc2])
            tmm = ("t", bd.cnt["t"])
            t1 = bd.emit("v", lambda p: p.tensor_scalar_mul(sts[:, 0:512], psB[:, 0:512], iscale), deps=[tmm])
            t2 = bd.emit("v", lambda p: p.tensor_tensor(
                out=sts[:, 512:1024], in0=sts[:, 0:512], in1=sts[:, 0:512], op=OP.mult), deps=[t1])
            t3 = bd.emit("v", lambda p: p.scalar_tensor_tensor(
                out=sts[:, 512:1024], in0=psB[:, 512:1024], scalar=iscale,
                in1=sts[:, 512:1024], op0=OP.mult, op1=OP.subtract), deps=[t2, tmm])
            t4 = bd.emit("s", lambda p: p.activation(
                out=sts[:, 1024:1536], in_=sts[:, 512:1024], func=AF.Ln, bias=cols[:, 204:205]), deps=[t3])
            t5 = bd.emit("s", lambda p: p.activation(
                out=sts[:, 512:1024], in_=sts[:, 1024:1536], func=AF.Exp, scale=-0.5), deps=[t4])
            outs = []
            for i in range(nchunks):
                ta = bd.emit("v", lambda p, i=i: p.tensor_tensor(
                    out=sts[:, 1024:1536], in0=src_ap_of(i), in1=sts[:, 0:512],
                    op=OP.subtract), deps=[t5] if i == 0 else [("v", bd.cnt["v"])])
                tb = bd.emit("v", lambda p, i=i: p.tensor_tensor(
                    out=lnc[:, 0:512], in0=sts[:, 1024:1536], in1=sts[:, 512:1024],
                    op=OP.mult), deps=[ta])
                outs.append(mod_apply(i, lnc[:, 0:512], tb))
            return outs

        for m in range(NIB):
            c, b = m // 2, m % 2
            bd.bar()
            for q in range(4):
                slot = stg_i[:, 0:1024]
                tl = bd.dma("y", lambda p, q=q, slot=slot: p.dma_start(
                    out=slot, in_=bass.AP(dram["x"], (b * T + c * TAU + q * 128) * D,
                                          [[D, 128], [1, D]])),
                    deps=[("t", bd.cnt["t"])] if q >= 1 else [])
                tts = []
                for dmt in range(8):
                    tts.append(bd.emit("t", lambda p, q=q, dmt=dmt, slot=slot: p.transpose(
                        psA[:, (q % 2) * 1024 + dmt * 128:(q % 2) * 1024 + dmt * 128 + 128],
                        slot[:, ts(dmt, 128)], idn[:]),
                        deps=[tl] + ([("s", bd.cnt["s"])] if q >= 2 else [])))
                bd.emit("s", lambda p, q=q: p.activation(
                    out=bass.AP(xres, q * 128, [[4096, 128], [512, 8], [1, 128]]),
                    in_=psA[:, (q % 2) * 1024:(q % 2) * 1024 + 1024], func=AF.Identity),
                    deps=tts)
            bd.bar()

            if MK_STOP == "int":
                break
            def mk_mod1(i, xhat_ap, dep):
                return bd.emit("v", lambda p, i=i: p.tensor_scalar(
                    out=h1[:, ts(i, 512)], in0=xhat_ap, scalar1=modcol(1, b, i),
                    scalar2=modcol(0, b, i), op0=OP.mult, op1=OP.add), deps=[dep])
            ln_feature_major(lambda i: xres[:, ts(i, 512)], None, 8, onds, mk_mod1, m)
            bd.bar()

            if MK_STOP == "h1":
                break
            prev_ev = {}
            prev_ld = {}
            for jt in range(32):
                slot = wi_r[:, ts(jt % 2, 1024)]
                tl = bd.dma("y", lambda p, jt=jt, slot=slot: p.dma_start(
                    out=slot, in_=bass.AP(sc["w_in"], jt * 8 * 128 * 128,
                                          [[128, 128], [128 * 128, 8], [1, 128]])),
                    deps=[prev_ld[jt - 2]] if jt >= 2 else [])
                mg = []
                for dc in range(8):
                    mg.append(bd.emit("t", lambda p, jt=jt, dc=dc, slot=slot: p.matmul(
                        psA[:, ts(jt % 4, 512)], slot[:, ts(dc, 128)], h1[:, ts(dc, 512)],
                        start=(dc == 0), stop=(dc == 7)),
                        deps=[tl] + ([prev_ev[jt - 4]] if jt >= 4 else [])))
                prev_ld[jt] = mg[-1]
                if jt < 16:
                    ev = bd.emit("s", lambda p, jt=jt: p.activation(
                        out=xin[:, ts(jt, 512)], in_=psA[:, ts(jt % 4, 512)],
                        func=AF.Identity, bias=col(C_BIN, jt)), deps=[mg[-1]])
                else:
                    ev = bd.emit("s", lambda p, jt=jt: p.activation(
                        out=zs[:, ts(jt - 16, 512)], in_=psA[:, ts(jt % 4, 512)],
                        func=AF.Silu, bias=col(C_BIN, jt)), deps=[mg[-1]])
                prev_ev[jt] = ev
            bd.bar()

            if MK_STOP == "xz":
                break
            for dc in range(16):
                bd.emit("t", lambda p, dc=dc: p.matmul(
                    psB[0:64, 0:512], wxs[:, dc * 96: dc * 96 + 64], xin[:, ts(dc, 512)],
                    start=(dc == 0), stop=(dc == 15)))
                bd.emit("t", lambda p, dc=dc: p.matmul(
                    psB[0:32, 512:1024], wxs[:, dc * 96 + 64: dc * 96 + 96],
                    xin[:, ts(dc, 512)], start=(dc == 0), stop=(dc == 15)))
            tmm = ("t", bd.cnt["t"])
            bd.emit("s", lambda p: p.activation(out=dtr_s[:], in_=psB[0:64, 0:512],
                                                func=AF.Identity), deps=[tmm])
            bd.emit("s", lambda p: p.activation(out=bcs[:], in_=psB[0:32, 512:1024],
                                                func=AF.Identity), deps=[tmm])
            bd.bar()

            if MK_STOP == "xdbl":
                break
            for dit in range(16):
                tm = bd.emit("t", lambda p, dit=dit: p.matmul(
                    psA[:, ts(dit % 2, 512)], wdts[:, ts(dit, 128)], dtr_s[:],
                    start=True, stop=True),
                    deps=[("s", bd.cnt["s"])])
                te = bd.emit("s", lambda p, dit=dit: p.activation(
                    out=sts[:, 1024:1536], in_=psA[:, ts(dit % 2, 512)],
                    func=AF.Exp, bias=col(C_BDT, dit)), deps=[tm])
                bd.emit("s", lambda p, dit=dit: p.activation(
                    out=dtb[:, ts(dit, 512)], in_=sts[:, 1024:1536],
                    func=AF.Ln, bias=cols[:, 205:206]), deps=[te])
                bd.emit("v", lambda p, dit=dit: p.tensor_scalar_mul(
                    yac[:, ts(dit, 512)], xin[:, ts(dit, 512)], col(C_D, dit)))
            bd.bar()

            if MK_STOP == "dt":
                break
            qc = [0]
            scan_tok = {}
            last_tree = [None]
            last_q = [None]
            for pas in range(2):
                reps = []
                for i in range(8):
                    n = pas * 8 + i
                    reps.append(bd.dma("g", lambda p, n=n, i=i: p.dma_start(
                        out=brep[:, ts(i, 512)],
                        in_=bass.AP(bcs, n * 512, [[512, 1], [0, 128], [1, 512]])),
                        deps=[("v", bd.cnt["v"])]))
                    reps.append(bd.dma("g", lambda p, n=n, i=i: p.dma_start(
                        out=crep[:, ts(i, 512)],
                        in_=bass.AP(bcs, (16 + n) * 512, [[512, 1], [0, 128], [1, 512]])),
                        deps=[("v", bd.cnt["v"])]))
                for dit in range(16):
                    tg = bd.emit("v", lambda p, dit=dit: p.tensor_tensor(
                        out=gms[:, ts(dit % 2, 512)], in0=dtb[:, ts(dit, 512)],
                        in1=xin[:, ts(dit, 512)], op=OP.mult),
                        deps=[("v", bd.cnt["v"])])
                    for qq in range(2):
                        slot = qc[0] % 2
                        off = slot * QLEN
                        exps = []
                        for s in range(QSEG):
                            n = pas * 8 + qq * 4 + s
                            dep = [scan_tok[qc[0] - 2]] if qc[0] >= 2 else []
                            exps.append(bd.emit("s", lambda p, dit=dit, n=n, off=off, s=s:
                                p.activation(
                                    out=abar[:, off + s * SEG + 1: off + s * SEG + SEG],
                                    in_=dtb[:, ts(dit, 512)], func=AF.Exp,
                                    scale=a_sb[:, dit * NS + n: dit * NS + n + 1]),
                                deps=dep))
                        depq = [scan_tok[qc[0] - 2]] if qc[0] >= 2 else []
                        if last_tree[0] is not None:
                            depq = depq + [last_tree[0]]
                        tr = bd.emit("v", lambda p, qq=qq, off=off, dit=dit: p.tensor_tensor(
                            out=bass.AP(rstp, off + 1, [[2 * QLEN, 128], [SEG, 4], [1, 512]]),
                            in0=bass.AP(gms, (dit % 2) * 512, [[1024, 128], [0, 4], [1, 512]]),
                            in1=brep[:, qq * 2048: qq * 2048 + 2048], op=OP.mult),
                            deps=[tg] + depq + reps)
                        sbase = b * 256 + dit * 16 + pas * 8 + qq * 4
                        tsi = bd.emit("v", lambda p, off=off, sbase=sbase: p.tensor_copy(
                            bass.AP(rstp, off, [[2 * QLEN, 128], [SEG, 4]]),
                            stat[:, sbase:sbase + 4]), deps=[tr])
                        tsc = bd.emit("v", lambda p, off=off, qq=qq: p.tensor_tensor_scan(
                            out=hstp[:, qq * QLEN: qq * QLEN + QLEN],
                            data0=abar[:, off: off + QLEN],
                            data1=rstp[:, off: off + QLEN],
                            initial=0.0, op0=OP.mult, op1=OP.add),
                            deps=exps + [tsi] + ([last_q[0]] if last_q[0] else []))
                        bd.emit("v", lambda p, qq=qq, sbase=sbase: p.tensor_copy(
                            stat[:, sbase:sbase + 4],
                            bass.AP(hstp, qq * QLEN + TAU, [[2 * QLEN, 128], [SEG, 4]])),
                            deps=[tsc])
                        scan_tok[qc[0]] = tsc
                        qc[0] += 1
                    tq = bd.emit("v", lambda p: p.tensor_tensor(
                        out=rstp[:, 0:4096],
                        in0=bass.AP(hstp, 1, [[2 * QLEN, 128], [SEG, 8], [1, 512]]),
                        in1=crep[:], op=OP.mult),
                        deps=[scan_tok[qc[0] - 1], scan_tok[qc[0] - 2], reps[-1]])
                    last_q[0] = tq
                    t_l1 = bd.emit("v", lambda p: p.tensor_tensor(
                        out=rstp[:, 0:2048], in0=rstp[:, 0:2048], in1=rstp[:, 2048:4096],
                        op=OP.add), deps=[tq])
                    t_l2 = bd.emit("v", lambda p: p.tensor_tensor(
                        out=rstp[:, 0:1024], in0=rstp[:, 0:1024], in1=rstp[:, 1024:2048],
                        op=OP.add), deps=[t_l1])
                    t_l3 = bd.emit("v", lambda p: p.tensor_tensor(
                        out=rstp[:, 0:512], in0=rstp[:, 0:512], in1=rstp[:, 512:1024],
                        op=OP.add), deps=[t_l2])
                    last_tree[0] = bd.emit("v", lambda p, dit=dit: p.tensor_tensor(
                        out=yac[:, ts(dit, 512)], in0=yac[:, ts(dit, 512)],
                        in1=rstp[:, 0:512], op=OP.add), deps=[t_l3])
            bd.bar()

            if MK_STOP == "scan":
                break
            for i in range(16):
                tc = bd.emit("s", lambda p, i=i: p.activation(
                    out=gms[:, 0:512], in_=yac[:, ts(i, 512)], func=AF.Square),
                    deps=[("t", bd.cnt["t"])])
                bd.emit("t", lambda p, i=i: p.matmul(
                    psB[:, 0:512], onds[:], yac[:, ts(i, 512)],
                    start=(i == 0), stop=(i == 15)))
                bd.emit("t", lambda p, i=i: p.matmul(
                    psB[:, 512:1024], onds[:], gms[:, 0:512],
                    start=(i == 0), stop=(i == 15)), deps=[tc])
            tmm = ("t", bd.cnt["t"])
            t1 = bd.emit("v", lambda p: p.tensor_scalar_mul(sts[:, 0:512], psB[:, 0:512], 1.0 / DI), deps=[tmm])
            t2 = bd.emit("v", lambda p: p.tensor_tensor(
                out=sts[:, 512:1024], in0=sts[:, 0:512], in1=sts[:, 0:512], op=OP.mult),
                deps=[t1])
            t3 = bd.emit("v", lambda p: p.scalar_tensor_tensor(
                out=sts[:, 512:1024], in0=psB[:, 512:1024], scalar=1.0 / DI,
                in1=sts[:, 512:1024], op0=OP.mult, op1=OP.subtract), deps=[t2, tmm])
            t4 = bd.emit("s", lambda p: p.activation(
                out=sts[:, 1024:1536], in_=sts[:, 512:1024], func=AF.Ln, bias=cols[:, 204:205]),
                deps=[t3])
            t5 = bd.emit("s", lambda p: p.activation(
                out=sts[:, 512:1024], in_=sts[:, 1024:1536], func=AF.Exp, scale=-0.5),
                deps=[t4])
            t6 = bd.emit("s", lambda p: p.activation(
                out=gms[:, 0:512], in_=sts[:, 0:512], func=AF.Identity), deps=[t5])
            t7 = bd.emit("s", lambda p: p.activation(
                out=gms[:, 512:1024], in_=sts[:, 512:1024], func=AF.Identity), deps=[t5, t6])
            for i in range(16):
                ta = bd.emit("v", lambda p, i=i: p.tensor_tensor(
                    out=gbb[:, 0:512], in0=yac[:, ts(i, 512)], in1=gms[:, 0:512],
                    op=OP.subtract), deps=[t7, ("v", bd.cnt["v"])])
                tb = bd.emit("v", lambda p, i=i: p.tensor_tensor(
                    out=gbb[:, 0:512], in0=gbb[:, 0:512], in1=gms[:, 512:1024],
                    op=OP.mult), deps=[ta])
                tc2 = bd.emit("v", lambda p, i=i: p.tensor_scalar(
                    out=gbb[:, 512:1024], in0=gbb[:, 0:512], scalar1=col(C_LNG, i),
                    scalar2=col(C_LNB, i), op0=OP.mult, op1=OP.add), deps=[tb])
                bd.emit("v", lambda p, i=i: p.tensor_tensor(
                    out=yac[:, ts(i, 512)], in0=gbb[:, 512:1024], in1=zs[:, ts(i, 512)],
                    op=OP.mult), deps=[tc2])
            bd.bar()

            if MK_STOP == "iln":
                break
            prev_ld = {}
            for dmt in range(8):
                slot = wo_r[:, ts(dmt % 2, 2048)]
                tl = bd.dma("y", lambda p, dmt=dmt, slot=slot: p.dma_start(
                    out=slot, in_=bass.AP(sc["w_out"], dmt * 16 * 128 * 128,
                                          [[128, 128], [128 * 128, 16], [1, 128]])),
                    deps=[prev_ld[dmt - 2]] if dmt >= 2 else [])
                mg = []
                for kc in range(16):
                    mg.append(bd.emit("t", lambda p, kc=kc, slot=slot, dmt=dmt: p.matmul(
                        psB[:, 1024 + (dmt % 2) * 512: 1536 + (dmt % 2) * 512],
                        slot[:, ts(kc, 128)], yac[:, ts(kc, 512)],
                        start=(kc == 0), stop=(kc == 15)),
                        deps=[tl] + ([("v", bd.cnt["v"])] if dmt >= 2 else [])))
                prev_ld[dmt] = mg[-1]
                tr1 = bd.emit("v", lambda p, dmt=dmt: p.scalar_tensor_tensor(
                    out=xres[:, ts(dmt, 512)],
                    in0=psB[:, 1024 + (dmt % 2) * 512: 1536 + (dmt % 2) * 512],
                    scalar=modcol(2, b, dmt), in1=xres[:, ts(dmt, 512)],
                    op0=OP.mult, op1=OP.add), deps=[mg[-1]])
                bd.emit("v", lambda p, dmt=dmt: p.tensor_scalar_add(
                    xres[:, ts(dmt, 512)], xres[:, ts(dmt, 512)], modcol(6, b, dmt)),
                    deps=[tr1])
            bd.bar()

            if MK_STOP == "wout":
                break
            def mk_mod2(i, xhat_ap, dep):
                return bd.emit("v", lambda p, i=i: p.tensor_scalar(
                    out=h2[:, ts(i, 512)], in0=xhat_ap, scalar1=modcol(4, b, i),
                    scalar2=modcol(3, b, i), op0=OP.mult, op1=OP.add), deps=[dep])
            ln_feature_major(lambda i: xres[:, ts(i, 512)], None, 8, onds, mk_mod2, m)
            bd.bar()

            if MK_STOP == "h2":
                break
            for dc in range(8):
                bd.emit("t", lambda p, dc=dc: p.matmul(
                    psB[0:4, 0:512], gws[:, ts(dc, 4)], h2[:, ts(dc, 512)],
                    start=(dc == 0), stop=(dc == 7)))
            tmm = ("t", bd.cnt["t"])
            tlg = bd.emit("s", lambda p: p.activation(
                out=sts[0:4, 1024:1536], in_=psB[0:4, 0:512], func=AF.Identity,
                bias=thr_s[:, 1:2]), deps=[tmm])
            for mt in range(4):
                slot = e1_r[:, ts(mt % 2, 1024)]
                tl = bd.dma("y", lambda p, mt=mt, slot=slot: p.dma_start(
                    out=slot, in_=bass.AP(sc["dp_w1"], mt * 8 * 128 * 128,
                                          [[128, 128], [128 * 128, 8], [1, 128]])),
                    deps=[("t", bd.cnt["t"])] if mt >= 2 else [])
                mg = []
                for dc in range(8):
                    mg.append(bd.emit("t", lambda p, mt=mt, dc=dc, slot=slot: p.matmul(
                        psA[:, ts(mt % 2, 512)], slot[:, ts(dc, 128)], h2[:, ts(dc, 512)],
                        start=(dc == 0), stop=(dc == 7)),
                        deps=[tl] + ([("s", bd.cnt["s"])] if mt >= 2 else [])))
                bd.emit("s", lambda p, mt=mt: p.activation(
                    out=hid[:, ts(mt, 512)], in_=psA[:, ts(mt % 2, 512)],
                    func=AF.Relu, bias=col(C_DPB1, mt)), deps=[mg[-1]])
            for kc in range(4):
                bd.emit("t", lambda p, kc=kc: p.matmul(
                    psB[0:1, 512:1024], dpw2s[:, kc:kc + 1], hid[:, ts(kc, 512)],
                    start=(kc == 0), stop=(kc == 3)), deps=[("s", bd.cnt["s"])])
            tmm = ("t", bd.cnt["t"])
            te1 = bd.emit("s", lambda p: p.activation(
                out=sts[0:1, 0:512], in_=psB[0:1, 512:1024], func=AF.Exp,
                bias=dpb2s[0:1, 0:1]), deps=[tmm])
            te2 = bd.emit("s", lambda p: p.activation(
                out=gsc[0:1, 512:1024], in_=sts[0:1, 0:512], func=AF.Ln, bias=cols[0:1, 205:206]),
                deps=[te1])
            trp = bd.dma("g", lambda p: p.dma_start(
                out=gsc[0:4, 0:512],
                in_=bass.AP(gsc, 512, [[1024, 1], [0, 4], [1, 512]])), deps=[te2])
            ts1 = bd.emit("v", lambda p: p.tensor_single_scalar(
                out=gbb[0:4, 0:512], in_=gsc[0:4, 0:512], scalar=thr_s[:, 0:1],
                op=OP.is_ge), deps=[trp])
            tk1 = bd.emit("t", lambda p: p.matmul(
                psB[0:4, 1024:1536], on4s[:], gbb[0:4, 0:512], start=True, stop=True),
                deps=[ts1])
            tk2 = bd.emit("v", lambda p: p.tensor_copy(gsc[0:4, 512:1024], psB[0:4, 1024:1536]),
                          deps=[tk1])
            tlb = bd.emit("s", lambda p: p.activation(
                out=gbb[0:4, 512:1024], in_=sts[0:4, 1024:1536], func=AF.Identity), deps=[tlg])
            td = bd.emit("t", lambda p: p.matmul(
                psA[0:16, 0:512], pcmps[:], gbb[0:4, 512:1024], start=True, stop=True),
                deps=[tlb, ("s", bd.cnt["s"]), ("v", bd.cnt["v"])])
            tgt = bd.emit("v", lambda p: p.tensor_single_scalar(
                out=gbb[0:16, 512:1024], in_=psA[0:16, 0:512], scalar=0.0,
                op=OP.is_lt), deps=[td])
            teq = bd.emit("v", lambda p: p.tensor_scalar(
                out=gsc[0:16, 0:512], in0=psA[0:16, 0:512], scalar1=0.0,
                scalar2=ltms[:, 0:1], op0=OP.is_equal, op1=OP.mult), deps=[td])
            tct = bd.emit("v", lambda p: p.tensor_tensor(
                out=gbb[0:16, 512:1024], in0=gbb[0:16, 512:1024],
                in1=gsc[0:16, 0:512], op=OP.add), deps=[tgt, teq])
            trk = bd.emit("t", lambda p: p.matmul(
                psA[0:4, 512:1024], prnks[:], gbb[0:16, 512:1024], start=True, stop=True),
                deps=[tct, ("s", bd.cnt["s"])])
            tin = bd.emit("v", lambda p: p.tensor_tensor(
                out=gsc[0:4, 0:512], in0=psA[0:4, 512:1024], in1=gsc[0:4, 512:1024],
                op=OP.is_lt), deps=[trk, tk2])
            tex = bd.emit("s", lambda p: p.activation(
                out=sts[0:4, 1024:1536], in_=sts[0:4, 1024:1536], func=AF.Exp), deps=[tlb, tin])
            twt = bd.emit("v", lambda p: p.tensor_tensor(
                out=gsc[0:4, 512:1024], in0=sts[0:4, 1024:1536], in1=gsc[0:4, 0:512], op=OP.mult),
                deps=[tex])
            twb = bd.emit("s", lambda p: p.activation(
                out=gbb[0:4, 0:512], in_=gsc[0:4, 512:1024], func=AF.Identity), deps=[twt])
            tz = bd.emit("t", lambda p: p.matmul(
                psB[0:4, 1024:1536], on4s[:], gbb[0:4, 0:512], start=True, stop=True),
                deps=[twb])
            trz = bd.emit("v", lambda p: p.reciprocal(gsc[0:4, 0:512], psB[0:4, 1024:1536]),
                          deps=[tz])
            tw = bd.emit("v", lambda p: p.tensor_tensor(
                out=gsc[0:4, 512:1024], in0=gsc[0:4, 512:1024], in1=gsc[0:4, 0:512],
                op=OP.mult), deps=[trz])
            twc = bd.emit("s", lambda p: p.activation(
                out=wrw[:], in_=gsc[0:4, 512:1024], func=AF.Identity), deps=[tw])
            for e in range(4):
                bd.dma("g", lambda p, e=e: p.dma_start(
                    out=wbc[:, ts(e, 512)],
                    in_=bass.AP(wrw, e * 512, [[512, 1], [0, 128], [1, 512]])),
                    deps=[twc])
            bd.bar()

            if MK_STOP == "gate":
                break
            prev_e1 = {}
            prev_e2 = {}
            for e in range(NE):
                for ht in range(16):
                    gi = e * 16 + ht
                    slot = e1_r[:, ts(gi % 2, 1024)]
                    tl = bd.dma("y", lambda p, e=e, ht=ht, slot=slot: p.dma_start(
                        out=slot, in_=bass.AP(sc["ew1"], (e * 16 + ht) * 8 * 128 * 128,
                                              [[128, 128], [128 * 128, 8], [1, 128]])),
                        deps=[prev_e1[gi - 2]] if gi >= 2 else [])
                    mg = []
                    for dc in range(8):
                        mg.append(bd.emit("t", lambda p, dc=dc, slot=slot, ht=ht: p.matmul(
                            psA[:, ts(ht % 2, 512)], slot[:, ts(dc, 128)],
                            h2[:, ts(dc, 512)], start=(dc == 0), stop=(dc == 7)),
                            deps=[tl] + ([("s", bd.cnt["s"])] if gi >= 2 else [])))
                    prev_e1[gi] = mg[-1]
                    bd.emit("s", lambda p, e=e, ht=ht: p.activation(
                        out=hid[:, ts(ht, 512)], in_=psA[:, ts(ht % 2, 512)],
                        func=AF.Relu, bias=col(C_EB1, e * 16 + ht)), deps=[mg[-1]])
                shid = ("s", bd.cnt["s"])
                for dmt in range(8):
                    gi = e * 8 + dmt
                    slot = wo_r[:, ts(gi % 2, 2048)]
                    tl = bd.dma("y", lambda p, e=e, dmt=dmt, slot=slot: p.dma_start(
                        out=slot, in_=bass.AP(sc["ew2"], (e * 8 + dmt) * 16 * 128 * 128,
                                              [[128, 128], [128 * 128, 16], [1, 128]])),
                        deps=[prev_e2[gi - 2]] if gi >= 2 else [])
                    mg = []
                    for kc in range(16):
                        mg.append(bd.emit("t", lambda p, kc=kc, slot=slot, dmt=dmt: p.matmul(
                            psB[:, 1024 + (dmt % 2) * 512: 1536 + (dmt % 2) * 512],
                            slot[:, ts(kc, 128)], hid[:, ts(kc, 512)],
                            start=(kc == 0), stop=(kc == 15)),
                            deps=[tl, shid] + ([("v", bd.cnt["v"])] if gi >= 2 else [])))
                    prev_e2[gi] = mg[-1]
                    if e == 0:
                        bd.emit("v", lambda p, e=e, dmt=dmt: p.scalar_tensor_tensor(
                            out=rstp[:, ts(dmt, 512)],
                            in0=psB[:, 1024 + (dmt % 2) * 512: 1536 + (dmt % 2) * 512],
                            scalar=col(C_EB2, e * 8 + dmt), in1=wbc[:, ts(e, 512)],
                            op0=OP.add, op1=OP.mult), deps=[mg[-1]])
                    else:
                        tc1 = bd.emit("v", lambda p, e=e, dmt=dmt: p.scalar_tensor_tensor(
                            out=gbb[:, 0:512],
                            in0=psB[:, 1024 + (dmt % 2) * 512: 1536 + (dmt % 2) * 512],
                            scalar=col(C_EB2, e * 8 + dmt), in1=wbc[:, ts(e, 512)],
                            op0=OP.add, op1=OP.mult), deps=[mg[-1]])
                        bd.emit("v", lambda p, dmt=dmt: p.tensor_tensor(
                            out=rstp[:, ts(dmt, 512)], in0=rstp[:, ts(dmt, 512)],
                            in1=gbb[:, 0:512], op=OP.add), deps=[tc1])
            bd.bar()

            if MK_STOP == "ffn":
                break
            for dmt in range(8):
                bd.emit("v", lambda p, dmt=dmt: p.scalar_tensor_tensor(
                    out=xres[:, ts(dmt, 512)], in0=rstp[:, ts(dmt, 512)],
                    scalar=modcol(5, b, dmt), in1=xres[:, ts(dmt, 512)],
                    op0=OP.mult, op1=OP.add))
            bd.bar()
            for q in range(4):
                tts = []
                for dmt in range(8):
                    tts.append(bd.emit("t", lambda p, q=q, dmt=dmt: p.transpose(
                        psA[:, (q % 2) * 1024 + dmt * 128:(q % 2) * 1024 + dmt * 128 + 128],
                        xres[:, dmt * 512 + q * 128: dmt * 512 + q * 128 + 128], idn[:]),
                        deps=[("s", bd.cnt["s"])] if q >= 2 else []))
                tev = bd.emit("s", lambda p, q=q: p.activation(
                    out=stg_o[:, 0:1024],
                    in_=psA[:, (q % 2) * 1024:(q % 2) * 1024 + 1024], func=AF.Identity),
                    deps=tts + ([("dy", bd.cnt["dy"])] if q >= 1 else []))
                bd.dma("y", lambda p, q=q: p.dma_start(
                    out=bass.AP(out_t, (b * T + c * TAU + q * 128) * D, [[D, 128], [1, D]]),
                    in_=stg_o[:, 0:1024]), deps=[tev])

        bd.bar()
        if MK_DBG:
            _dt = {t.name: t for t in [stg_i, stg_o, xres, h1, xin, zs, dtb, yac,
                   abar, rstp, hstp, brep, crep, stat, h2, hid, sts, gms, lnc,
                   gsc, gbb, wbc, wrw, dtr_s, bcs, wxs, wdts, a_sb, cols, mods,
                   modp, cTs]}[MK_DBG]
            _nf = _dt.shape[1]
            bd.dma("g", lambda p: p.dma_start(
                out=dbg_t.ap()[0:_dt.shape[0], 0:_nf], in_=_dt[:]))
            bd.bar()
        final_dy = bd.cnt["dy"]
        final_dg = bd.cnt["dg"]
        bd.raw("y", lambda p: (p.wait_ge(bd.sem["dy"], final_dy),
                               p.wait_ge(bd.sem["dg"], final_dg)))

    return nc


_CACHE = {}


def _host_prep(inputs):
    f = np.float32
    W_in = np.ascontiguousarray(inputs["L0_attn_W_in"], f)
    w_in = np.ascontiguousarray(
        W_in.reshape(8, 128, 32, 128).transpose(2, 0, 1, 3))
    w_x = np.ascontiguousarray(
        inputs["L0_attn_W_x"].reshape(16, 128, 96).transpose(1, 0, 2).reshape(128, 1536))
    w_dt = np.ascontiguousarray(inputs["L0_attn_W_dt"])
    w_out = np.ascontiguousarray(
        inputs["L0_attn_W_out"].reshape(16, 128, 8, 128).transpose(2, 0, 1, 3))
    ew1 = np.ascontiguousarray(
        inputs["L0_ffn_ew1"].reshape(NE, 8, 128, 16, 128).transpose(0, 3, 1, 2, 4))
    ew2 = np.ascontiguousarray(
        inputs["L0_ffn_ew2"].reshape(NE, 16, 128, 8, 128).transpose(0, 3, 1, 2, 4))
    wmod = np.ascontiguousarray(
        inputs["L0_wmod"].reshape(8, 128, 48, 128).transpose(2, 0, 1, 3))
    dp_w1 = np.ascontiguousarray(
        inputs["L0_ffn_dp_w1"].reshape(8, 128, 4, 128).transpose(2, 0, 1, 3))
    dp_w2 = np.ascontiguousarray(inputs["L0_ffn_dp_w2"].reshape(4, 128).T)
    gate_w = np.ascontiguousarray(
        inputs["L0_ffn_gate_w"].reshape(8, 128, 4).transpose(1, 0, 2).reshape(128, 32))
    a_log = np.ascontiguousarray(
        inputs["L0_attn_A_log"].reshape(16, 128, 16).transpose(1, 0, 2).reshape(128, 256))

    colpack = np.zeros((128, 208), f)
    colpack[:, 0:32] = inputs["L0_attn_b_in"].reshape(32, 128).T
    colpack[:, 32:48] = inputs["L0_attn_b_dt"].reshape(16, 128).T
    colpack[:, 48:64] = inputs["L0_attn_D"].reshape(16, 128).T
    colpack[:, 64:80] = inputs["L0_attn_ln_g"].reshape(16, 128).T
    colpack[:, 80:96] = inputs["L0_attn_ln_b"].reshape(16, 128).T
    colpack[:, 96:104] = inputs["L0_attn_b_out"].reshape(8, 128).T
    colpack[:, 104:168] = inputs["L0_ffn_eb1"].reshape(4 * 16, 128).T
    colpack[:, 168:200] = inputs["L0_ffn_eb2"].reshape(4 * 8, 128).T
    colpack[:, 200:204] = inputs["L0_ffn_dp_b1"].reshape(4, 128).T
    colpack[:, 204] = EPS
    colpack[:, 205] = 1.0

    thr = np.zeros((4, 2), f)
    thr[:, 0] = [-1e30, 0.75, 1.25, 1.75]
    thr[:, 1] = inputs["L0_ffn_gate_b"]
    ltm = np.zeros((16, 1), f)
    pcmp = np.zeros((4, 16), f)
    for e in range(4):
        for e2 in range(4):
            pcmp[e, 4 * e + e2] += 1.0
            pcmp[e2, 4 * e + e2] -= 1.0
            if e2 < e:
                ltm[4 * e + e2, 0] = 1.0
    prnk = np.zeros((16, 4), f)
    for e in range(4):
        prnk[4 * e: 4 * e + 4, e] = 1.0
    on4s = np.ones((4, 4), f)
    ident = np.eye(128, dtype=f)
    onds = np.ones((128, 128), f)
    dpb2 = inputs["L0_ffn_dp_b2"].reshape(1, 1).astype(f)

    shared = dict(w_in=w_in, w_x=w_x, w_dt=w_dt, w_out=w_out, ew1=ew1, ew2=ew2,
                  wmod=wmod, dp_w1=dp_w1, dp_w2=dp_w2, gate_w=gate_w,
                  a_log=a_log, colpack=colpack, thrs=thr, ltm=ltm, pcmp=pcmp,
                  prnk=prnk, ones4=on4s, ident=ident, onesd=onds, dpb2=dpb2)

    x = np.ascontiguousarray(inputs["L0_attn_x"], f)
    c_all = np.ascontiguousarray(inputs["L0_c"], f)
    in_maps = []
    for core in range(8):
        im = dict(shared)
        im["x"] = np.ascontiguousarray(x[2 * core: 2 * core + 2])
        cc = c_all[2 * core: 2 * core + 2]
        im["cT"] = np.ascontiguousarray(cc.T.reshape(8, 128, 2).transpose(1, 0, 2).reshape(128, 16))
        in_maps.append(im)
    return in_maps


def kernel(**inputs):
    in_maps = _host_prep(inputs)
    if "nc" not in _CACHE:
        _CACHE["nc"] = build_nc()
    nc = _CACHE["nc"]
    res = run_bass_kernel_spmd(nc, in_maps, core_ids=list(range(8)))
    out = np.empty((16, T, D), np.float32)
    for core in range(8):
        out[2 * core: 2 * core + 2] = res.results[core]["out"]
    return out


# version 05
# speedup vs baseline: 1.0893x; 1.0893x over previous
import os
import numpy as np
import concourse.bass as bass
import concourse.mybir as mybir
from concourse.bass_utils import run_bass_kernel_spmd
from contextlib import ExitStack

F32 = mybir.dt.float32
BF16 = mybir.dt.bfloat16
F16 = mybir.dt.float16
AF = mybir.ActivationFunctionType
OP = mybir.AluOpType

D, DI, NS, DTR = 1024, 2048, 16, 64
NB, T, NCH, TAU = 2, 2048, 4, 512
NE, HF, DP = 4, 2048, 512
SEG = TAU + 1
QSEG = 4
QLEN = QSEG * SEG
EPS = 1e-5
NIB = NCH * NB
MK_STOP = os.environ.get("MK_STOP", "")
MK_DBG = os.environ.get("MK_DBG", "")


def ts(i, n):
    return slice(i * n, (i + 1) * n)


class BD:

    def __init__(self, nc, block, es):
        self.nc, self.block = nc, block
        self.sem = {ch: es.enter_context(nc.semaphore("sem_" + ch))
                    for ch in ("v", "s", "t", "g", "dy", "dg")}
        self.cnt = {ch: 0 for ch in self.sem}
        self.waited = {}
        self.secs = {"v": block.vector, "s": block.scalar, "t": block.tensor,
                     "g": block.gpsimd, "y": block.sync}
        self.barrier = []
        self.buf = {e: [] for e in ("v", "s", "t", "g", "y")}

    def bar(self):
        self.barrier = [(ch, self.cnt[ch]) for ch in self.sem]

    def _calc_waits(self, eng, deps):
        need = {}
        for (ch, val) in list(deps) + self.barrier:
            if ch in ("dy", "dg") and val > 0:
                val = self.cnt[ch]
            if val > 0 and need.get(ch, 0) < val:
                need[ch] = val
        waits = []
        for ch, val in sorted(need.items()):
            key = (eng, ch)
            if self.waited.get(key, 0) >= val:
                continue
            waits.append((self.sem[ch], val))
            self.waited[key] = val
        return waits

    def emit(self, eng, fn, deps=()):
        waits = self._calc_waits(eng, list(deps) + [(eng, self.cnt[eng])])
        sem = self.sem[eng]
        self.buf[eng].append((waits, fn, sem, 1))
        self.cnt[eng] += 1
        return (eng, self.cnt[eng])

    def dma(self, issuer, fn, deps=()):
        ch = "dy" if issuer == "y" else "dg"
        waits = self._calc_waits(issuer, deps)
        self.buf[issuer].append((waits, fn, self.sem[ch], 16))
        self.cnt[ch] += 16
        return (ch, self.cnt[ch])

    def raw(self, eng, fn):
        self.buf[eng].append((None, fn, None, 0))

    def flush(self):
        for eng, items in self.buf.items():
            if not items:
                continue

            def body(p, items=items):
                for waits, fn, sem, inc in items:
                    if waits is None:
                        fn(p)
                        continue
                    for s, v in waits:
                        p.wait_ge(s, v)
                    ins = fn(p)
                    if inc:
                        ins.then_inc(sem, inc)
            self.secs[eng](body)
        self.buf = {e: [] for e in self.buf}


def build_nc():
    nc = bass.Bass("TRN2", target_bir_lowering=False, dynamic_dma_scratch_size=8192)
    dram = {}

    def din(name, shape):
        dram[name] = nc.dram_tensor(name, list(shape), F32, kind="ExternalInput")
        return dram[name]

    din("x", (NB, T, D))
    din("w_in", (32, 8, 128, 128))
    din("w_x", (128, 1536))
    din("w_dt", (64, 2048))
    din("w_out", (8, 16, 128, 128))
    din("ew1", (NE, 16, 8, 128, 128))
    din("ew2", (NE, 8, 16, 128, 128))
    din("wmod", (48, 8, 128, 128))
    din("dp_w1", (4, 8, 128, 128))
    din("dp_w2", (128, 4))
    din("gate_w", (128, 32))
    din("a_log", (128, 16 * NS))
    din("colpack", (128, 208))
    din("thrs", (4, 2))
    din("ltm", (16, 1))
    din("pcmp", (4, 16))
    din("prnk", (16, 4))
    din("ones4", (4, 4))
    din("cT", (128, 8 * NB))
    din("ident", (128, 128))
    din("onesd", (128, 128))
    din("dpb2", (1, 1))
    out_t = nc.dram_tensor("out", [NB, T, D], F32, kind="ExternalOutput")
    dbg_t = nc.dram_tensor("dbg", [128, 8208], F32, kind="ExternalOutput")

    sc = {}
    sc["w_in"] = nc.dram_tensor("s_w_in", [32, 8, 128, 128], BF16)
    sc["w_out"] = nc.dram_tensor("s_w_out", [8, 16, 128, 128], BF16)
    sc["ew1"] = nc.dram_tensor("s_ew1", [NE, 16, 8, 128, 128], BF16)
    sc["ew2"] = nc.dram_tensor("s_ew2", [NE, 8, 16, 128, 128], BF16)
    sc["dp_w1"] = nc.dram_tensor("s_dp_w1", [4, 8, 128, 128], BF16)

    C_BIN, C_BDT, C_D, C_LNG, C_LNB, C_BOUT, C_EB1, C_EB2, C_DPB1 = (
        0, 32, 48, 64, 80, 96, 104, 168, 200)

    es = ExitStack()

    def sb(name, p, f, dt):
        return es.enter_context(nc.sbuf_tensor(name, [p, f], dt))

    stg_i = sb("stg_i", 128, 1024, F32)
    stg_o = sb("stg_o", 128, 1024, F32)
    xres = sb("xres", 128, 8 * 512, F32)
    h1 = sb("h1", 128, 8 * 512, BF16)
    xin = sb("xin", 128, 16 * 512, BF16)
    zs = sb("zs", 128, 16 * 512, BF16)
    dtb = sb("dtb", 128, 16 * 512, BF16)
    yac = sb("yac", 128, 16 * 512, BF16)
    abar = sb("abar", 128, 2 * QLEN, F16)
    rstp = sb("rstp", 128, 2 * QLEN, BF16)
    hstp = sb("hstp", 128, 2 * QLEN, BF16)
    brep = sb("brep", 128, 8 * 512, BF16)
    crep = sb("crep", 128, 8 * 512, BF16)
    stat = sb("stat", 128, NB * 16 * NS, F32)
    h2 = sb("h2", 128, 8 * 512, BF16)
    hid = sb("hid", 128, 16 * 512, BF16)
    sts = sb("sts", 128, 3 * 512, F32)
    gms = sb("gms", 128, 2 * 512, BF16)
    lnc = sb("lnc", 128, 2 * 512, BF16)
    gsc = sb("gsc", 128, 2 * 512, F32)
    gbb = sb("gbb", 128, 2 * 512, BF16)
    wbc = sb("wbc", 128, 4 * 512, BF16)
    wrw = sb("wrw", 4, 512, BF16)
    dtr_s = sb("dtr_s", 64, 512, BF16)
    bcs = sb("bcs", 32, 512, BF16)
    wxs = sb("wxs", 128, 16 * 96, BF16)
    wdts = sb("wdts", 64, 16 * 128, BF16)
    dpw2s = sb("dpw2s", 128, 4, BF16)
    gws = sb("gws", 128, 8 * 4, BF16)
    a_sb = sb("a_sb", 128, 16 * NS, F32)
    idn = sb("idn", 128, 128, F32)
    onds = sb("onds", 128, 128, BF16)
    on4s = sb("on4s", 4, 4, BF16)
    cols = sb("cols", 128, 208, F32)
    thr_s = sb("thr_s", 4, 2, F32)
    ltms = sb("ltms", 16, 1, F32)
    pcmps = sb("pcmps", 4, 16, BF16)
    prnks = sb("prnks", 16, 4, BF16)
    mods = sb("mods", 128, 96, F32)
    modp = sb("modp", 128, 112, F32)
    cTs = sb("cTs", 128, 8 * NB, F32)
    dpb2s = sb("dpb2s", 1, 1, F32)
    wi_r = sb("wi_r", 128, 2 * 1024, BF16)
    wo_r = sb("wo_r", 128, 2 * 2048, BF16)
    e1_r = sb("e1_r", 128, 2 * 1024, BF16)

    psA = es.enter_context(nc.psum_tensor("psA", [128, 2048], F32))
    psB = es.enter_context(nc.psum_tensor("psB", [128, 2048], F32))

    def col(base, i):
        return cols[:, base + i: base + i + 1]

    with es, nc.Block() as block:
        bd = BD(nc, block, es)

        def modcol(slot, b, dmt):
            c = slot * 16 + dmt * 2 + b
            return modp[:, c:c + 1]

        for name, dst in (("ident", idn), ("colpack", cols), ("thrs", thr_s),
                          ("ltm", ltms), ("dpb2", dpb2s), ("a_log", a_sb)):
            bd.dma("y", lambda p, n=name, d=dst: p.dma_start(out=d[:], in_=dram[n].ap()))
        bd.dma("y", lambda p: p.dma_start(out=mods[:, 0:16], in_=dram["cT"].ap()))
        for name, dst in (("onesd", onds), ("ones4", on4s),
                          ("pcmp", pcmps), ("prnk", prnks), ("w_x", wxs),
                          ("w_dt", wdts), ("dp_w2", dpw2s), ("gate_w", gws)):
            t = dram[name]
            n_el = int(np.prod(t.shape))
            pf = n_el // dst.shape[0]
            bd.dma("g", lambda p, t=t, d=dst, pf=pf: p.dma_start(
                out=d[:, 0:pf], in_=bass.AP(t, 0, [[pf, d.shape[0]], [1, pf]])))
        bd.bar()

        bd.emit("s", lambda p: p.activation(out=a_sb[:], in_=a_sb[:], func=AF.Exp))
        bd.emit("v", lambda p: p.tensor_scalar_mul(a_sb[:], a_sb[:], -1.0),
                deps=[("s", bd.cnt["s"])])
        tcs = bd.emit("s", lambda p: p.activation(out=cTs[:], in_=mods[:, 0:8 * NB], func=AF.Silu))
        bd.emit("v", lambda p: p.memset(stat[:], 0.0))
        bd.emit("v", lambda p: p.memset(abar[:], 0.0))
        bd.emit("v", lambda p: p.memset(rstp[:], 0.0))
        bd.emit("v", lambda p: p.memset(hstp[:], 0.0))
        bd.bar()

        prev_mm = {}
        wmf = (stg_i, stg_o)
        for mt in range(48):
            buf = wmf[mt % 2][:, 0:1024]
            deps = [prev_mm[mt - 2]] if mt >= 2 else []
            tw = bd.dma("y", lambda p, mt=mt, buf=buf: p.dma_start(
                out=buf[:, 0:1024],
                in_=bass.AP(dram["wmod"], mt * 8 * 128 * 128,
                            [[128, 128], [128 * 128, 8], [1, 128]])), deps=deps)
            for dc in range(8):
                tm = bd.emit("t", lambda p, mt=mt, dc=dc, buf=buf: p.matmul(
                    psB[:, 2 * mt: 2 * mt + 2], buf[:, ts(dc, 128)],
                    cTs[:, 2 * dc: 2 * dc + 2], start=(dc == 0), stop=(dc == 7)),
                    deps=[tw, tcs])
            prev_mm[mt] = tm
        bd.bar()
        bd.emit("s", lambda p: p.activation(out=mods[:], in_=psB[:, 0:96], func=AF.Identity))
        bd.bar()
        bd.emit("v", lambda p: p.tensor_copy(modp[:, 0:16], mods[:, 0:16]))
        bd.emit("v", lambda p: p.tensor_scalar_add(modp[:, 16:32], mods[:, 16:32], 1.0))
        bd.emit("v", lambda p: p.tensor_copy(modp[:, 48:64], mods[:, 48:64]))
        bd.emit("v", lambda p: p.tensor_scalar_add(modp[:, 64:80], mods[:, 64:80], 1.0))
        for src, dst in ((32, 32), (80, 80)):
            bd.emit("v", lambda p, src=src: p.tensor_scalar(
                out=sts[:, 0:16], in0=mods[:, src:src + 16], scalar1=0.0,
                scalar2=2.0, op0=OP.is_ge, op1=OP.mult))
            bd.emit("v", lambda p: p.tensor_scalar_add(sts[:, 0:16], sts[:, 0:16], -1.0),
                    deps=[("v", bd.cnt["v"])])
            bd.emit("v", lambda p, src=src: p.tensor_tensor(
                out=sts[:, 16:32], in0=mods[:, src:src + 16], in1=sts[:, 0:16],
                op=OP.mult), deps=[("v", bd.cnt["v"])])
            bd.emit("v", lambda p: p.tensor_scalar_max(sts[:, 16:32], sts[:, 16:32], 0.25),
                    deps=[("v", bd.cnt["v"])])
            bd.emit("v", lambda p, dst=dst: p.tensor_tensor(
                out=modp[:, dst:dst + 16], in0=sts[:, 16:32], in1=sts[:, 0:16],
                op=OP.mult), deps=[("v", bd.cnt["v"])])
        for b in range(NB):
            for dmt in range(8):
                bd.emit("v", lambda p, b=b, dmt=dmt: p.tensor_tensor(
                    out=modcol(6, b, dmt), in0=modcol(2, b, dmt),
                    in1=col(C_BOUT, dmt), op=OP.mult))
        bd.bar()

        def cache(name, n_chunks, chunk_elems):
            t = dram[name]
            st = sc[name]
            pf = chunk_elems // 128
            prev = {}
            for i in range(n_chunks):
                slot = wo_r[:, ts(i % 2, 2048)][:, 0:pf]
                deps = [prev[i - 2]] if i >= 2 else []
                tl = bd.dma("g", lambda p, t=t, i=i, slot=slot, pf=pf: p.dma_start(
                    out=slot, in_=bass.AP(t, i * pf * 128, [[pf, 128], [1, pf]])),
                    deps=deps)
                tst = bd.dma("y", lambda p, st=st, i=i, slot=slot, pf=pf: p.dma_start(
                    out=bass.AP(st, i * pf * 128, [[pf, 128], [1, pf]]), in_=slot),
                    deps=[tl])
                prev[i] = tst

        cache("w_in", 32, 8 * 128 * 128)
        cache("w_out", 8, 16 * 128 * 128)
        cache("ew1", 64, 8 * 128 * 128)
        cache("ew2", 32, 16 * 128 * 128)
        cache("dp_w1", 4, 8 * 128 * 128)
        bd.bar()

        def ln_feature_major(src_ap_of, dst_of, nchunks, ones_t, mod_apply, ib_tag, iscale=1.0 / D):
            for i in range(nchunks):
                tc1 = bd.emit("s", lambda p, i=i: p.activation(
                    out=lnc[:, 0:512], in_=src_ap_of(i), func=AF.Identity),
                    deps=[("t", bd.cnt["t"])])
                tc2 = bd.emit("s", lambda p, i=i: p.activation(
                    out=lnc[:, 512:1024], in_=src_ap_of(i), func=AF.Square),
                    deps=[tc1])
                bd.emit("t", lambda p, i=i: p.matmul(
                    psB[:, 0:512], ones_t[:], lnc[:, 0:512],
                    start=(i == 0), stop=(i == nchunks - 1)), deps=[tc1])
                bd.emit("t", lambda p, i=i: p.matmul(
                    psB[:, 512:1024], ones_t[:], lnc[:, 512:1024],
                    start=(i == 0), stop=(i == nchunks - 1)), deps=[tc2])
            tmm = ("t", bd.cnt["t"])
            t1 = bd.emit("v", lambda p: p.tensor_scalar_mul(sts[:, 0:512], psB[:, 0:512], iscale), deps=[tmm])
            t2 = bd.emit("v", lambda p: p.tensor_tensor(
                out=sts[:, 512:1024], in0=sts[:, 0:512], in1=sts[:, 0:512], op=OP.mult), deps=[t1])
            t3 = bd.emit("v", lambda p: p.scalar_tensor_tensor(
                out=sts[:, 512:1024], in0=psB[:, 512:1024], scalar=iscale,
                in1=sts[:, 512:1024], op0=OP.mult, op1=OP.subtract), deps=[t2, tmm])
            t4 = bd.emit("s", lambda p: p.activation(
                out=sts[:, 1024:1536], in_=sts[:, 512:1024], func=AF.Ln, bias=cols[:, 204:205]), deps=[t3])
            t5 = bd.emit("s", lambda p: p.activation(
                out=sts[:, 512:1024], in_=sts[:, 1024:1536], func=AF.Exp, scale=-0.5), deps=[t4])
            outs = []
            for i in range(nchunks):
                ta = bd.emit("v", lambda p, i=i: p.tensor_tensor(
                    out=sts[:, 1024:1536], in0=src_ap_of(i), in1=sts[:, 0:512],
                    op=OP.subtract), deps=[t5] if i == 0 else [("v", bd.cnt["v"])])
                tb = bd.emit("v", lambda p, i=i: p.tensor_tensor(
                    out=lnc[:, 0:512], in0=sts[:, 1024:1536], in1=sts[:, 512:1024],
                    op=OP.mult), deps=[ta])
                outs.append(mod_apply(i, lnc[:, 0:512], tb))
            return outs

        for m in range(NIB):
            c, b = m // 2, m % 2
            bd.flush()
            bd.bar()
            for q in range(4):
                slot = stg_i[:, 0:1024]
                tl = bd.dma("y", lambda p, q=q, slot=slot, b=b, c=c: p.dma_start(
                    out=slot, in_=bass.AP(dram["x"], (b * T + c * TAU + q * 128) * D,
                                          [[D, 128], [1, D]])),
                    deps=[("t", bd.cnt["t"])] if q >= 1 else [])
                tts = []
                for dmt in range(8):
                    tts.append(bd.emit("t", lambda p, q=q, dmt=dmt, slot=slot: p.transpose(
                        psA[:, (q % 2) * 1024 + dmt * 128:(q % 2) * 1024 + dmt * 128 + 128],
                        slot[:, ts(dmt, 128)], idn[:]),
                        deps=[tl] + ([("s", bd.cnt["s"])] if q >= 2 else [])))
                bd.emit("s", lambda p, q=q: p.activation(
                    out=bass.AP(xres, q * 128, [[4096, 128], [512, 8], [1, 128]]),
                    in_=psA[:, (q % 2) * 1024:(q % 2) * 1024 + 1024], func=AF.Identity),
                    deps=tts)
            bd.bar()

            if MK_STOP == "int":
                break
            def mk_mod1(i, xhat_ap, dep):
                return bd.emit("v", lambda p, i=i, b=b: p.tensor_scalar(
                    out=h1[:, ts(i, 512)], in0=xhat_ap, scalar1=modcol(1, b, i),
                    scalar2=modcol(0, b, i), op0=OP.mult, op1=OP.add), deps=[dep])
            ln_feature_major(lambda i: xres[:, ts(i, 512)], None, 8, onds, mk_mod1, m)
            bd.bar()

            if MK_STOP == "h1":
                break
            prev_ev = {}
            prev_ld = {}
            for jt in range(32):
                slot = wi_r[:, ts(jt % 2, 1024)]
                tl = bd.dma("y", lambda p, jt=jt, slot=slot: p.dma_start(
                    out=slot, in_=bass.AP(sc["w_in"], jt * 8 * 128 * 128,
                                          [[128, 128], [128 * 128, 8], [1, 128]])),
                    deps=[prev_ld[jt - 2]] if jt >= 2 else [])
                mg = []
                for dc in range(8):
                    mg.append(bd.emit("t", lambda p, jt=jt, dc=dc, slot=slot: p.matmul(
                        psA[:, ts(jt % 4, 512)], slot[:, ts(dc, 128)], h1[:, ts(dc, 512)],
                        start=(dc == 0), stop=(dc == 7)),
                        deps=[tl] + ([prev_ev[jt - 4]] if jt >= 4 else [])))
                prev_ld[jt] = mg[-1]
                if jt < 16:
                    ev = bd.emit("s", lambda p, jt=jt: p.activation(
                        out=xin[:, ts(jt, 512)], in_=psA[:, ts(jt % 4, 512)],
                        func=AF.Identity, bias=col(C_BIN, jt)), deps=[mg[-1]])
                else:
                    ev = bd.emit("s", lambda p, jt=jt: p.activation(
                        out=zs[:, ts(jt - 16, 512)], in_=psA[:, ts(jt % 4, 512)],
                        func=AF.Silu, bias=col(C_BIN, jt)), deps=[mg[-1]])
                prev_ev[jt] = ev
            bd.bar()

            if MK_STOP == "xz":
                break
            for dc in range(16):
                bd.emit("t", lambda p, dc=dc: p.matmul(
                    psB[0:64, 0:512], wxs[:, dc * 96: dc * 96 + 64], xin[:, ts(dc, 512)],
                    start=(dc == 0), stop=(dc == 15)))
                bd.emit("t", lambda p, dc=dc: p.matmul(
                    psB[0:32, 512:1024], wxs[:, dc * 96 + 64: dc * 96 + 96],
                    xin[:, ts(dc, 512)], start=(dc == 0), stop=(dc == 15)))
            tmm = ("t", bd.cnt["t"])
            bd.emit("s", lambda p: p.activation(out=dtr_s[:], in_=psB[0:64, 0:512],
                                                func=AF.Identity), deps=[tmm])
            bd.emit("s", lambda p: p.activation(out=bcs[:], in_=psB[0:32, 512:1024],
                                                func=AF.Identity), deps=[tmm])
            bd.bar()

            if MK_STOP == "xdbl":
                break
            for dit in range(16):
                tm = bd.emit("t", lambda p, dit=dit: p.matmul(
                    psA[:, ts(dit % 2, 512)], wdts[:, ts(dit, 128)], dtr_s[:],
                    start=True, stop=True),
                    deps=[("s", bd.cnt["s"])])
                te = bd.emit("s", lambda p, dit=dit: p.activation(
                    out=sts[:, 1024:1536], in_=psA[:, ts(dit % 2, 512)],
                    func=AF.Exp, bias=col(C_BDT, dit)), deps=[tm])
                bd.emit("s", lambda p, dit=dit: p.activation(
                    out=dtb[:, ts(dit, 512)], in_=sts[:, 1024:1536],
                    func=AF.Ln, bias=cols[:, 205:206]), deps=[te])
                bd.emit("v", lambda p, dit=dit: p.tensor_scalar_mul(
                    yac[:, ts(dit, 512)], xin[:, ts(dit, 512)], col(C_D, dit)))
            bd.bar()

            if MK_STOP == "dt":
                break
            qc = [0]
            scan_tok = {}
            last_tree = [None]
            last_q = [None]
            for pas in range(2):
                reps = []
                for i in range(8):
                    n = pas * 8 + i
                    reps.append(bd.dma("g", lambda p, n=n, i=i: p.dma_start(
                        out=brep[:, ts(i, 512)],
                        in_=bass.AP(bcs, n * 512, [[512, 1], [0, 128], [1, 512]])),
                        deps=[("v", bd.cnt["v"])]))
                    reps.append(bd.dma("g", lambda p, n=n, i=i: p.dma_start(
                        out=crep[:, ts(i, 512)],
                        in_=bass.AP(bcs, (16 + n) * 512, [[512, 1], [0, 128], [1, 512]])),
                        deps=[("v", bd.cnt["v"])]))
                for dit in range(16):
                    tg = bd.emit("v", lambda p, dit=dit: p.tensor_tensor(
                        out=gms[:, ts(dit % 2, 512)], in0=dtb[:, ts(dit, 512)],
                        in1=xin[:, ts(dit, 512)], op=OP.mult),
                        deps=[("v", bd.cnt["v"])])
                    for qq in range(2):
                        slot = qc[0] % 2
                        off = slot * QLEN
                        exps = []
                        for s in range(QSEG):
                            n = pas * 8 + qq * 4 + s
                            dep = [scan_tok[qc[0] - 2]] if qc[0] >= 2 else []
                            exps.append(bd.emit("s", lambda p, dit=dit, n=n, off=off, s=s:
                                p.activation(
                                    out=abar[:, off + s * SEG + 1: off + s * SEG + SEG],
                                    in_=dtb[:, ts(dit, 512)], func=AF.Exp,
                                    scale=a_sb[:, dit * NS + n: dit * NS + n + 1]),
                                deps=dep))
                        depq = [scan_tok[qc[0] - 2]] if qc[0] >= 2 else []
                        if last_tree[0] is not None:
                            depq = depq + [last_tree[0]]
                        tr = bd.emit("v", lambda p, qq=qq, off=off, dit=dit: p.tensor_tensor(
                            out=bass.AP(rstp, off + 1, [[2 * QLEN, 128], [SEG, 4], [1, 512]]),
                            in0=bass.AP(gms, (dit % 2) * 512, [[1024, 128], [0, 4], [1, 512]]),
                            in1=brep[:, qq * 2048: qq * 2048 + 2048], op=OP.mult),
                            deps=[tg] + depq + reps)
                        sbase = b * 256 + dit * 16 + pas * 8 + qq * 4
                        tsi = bd.emit("v", lambda p, off=off, sbase=sbase: p.tensor_copy(
                            bass.AP(rstp, off, [[2 * QLEN, 128], [SEG, 4]]),
                            stat[:, sbase:sbase + 4]), deps=[tr])
                        tsc = bd.emit("v", lambda p, off=off, qq=qq: p.tensor_tensor_scan(
                            out=hstp[:, qq * QLEN: qq * QLEN + QLEN],
                            data0=abar[:, off: off + QLEN],
                            data1=rstp[:, off: off + QLEN],
                            initial=0.0, op0=OP.mult, op1=OP.add),
                            deps=exps + [tsi] + ([last_q[0]] if last_q[0] else []))
                        bd.emit("v", lambda p, qq=qq, sbase=sbase: p.tensor_copy(
                            stat[:, sbase:sbase + 4],
                            bass.AP(hstp, qq * QLEN + TAU, [[2 * QLEN, 128], [SEG, 4]])),
                            deps=[tsc])
                        scan_tok[qc[0]] = tsc
                        qc[0] += 1
                    tq = bd.emit("v", lambda p: p.tensor_tensor(
                        out=rstp[:, 0:4096],
                        in0=bass.AP(hstp, 1, [[2 * QLEN, 128], [SEG, 8], [1, 512]]),
                        in1=crep[:], op=OP.mult),
                        deps=[scan_tok[qc[0] - 1], scan_tok[qc[0] - 2], reps[-1]])
                    last_q[0] = tq
                    t_l1 = bd.emit("v", lambda p: p.tensor_tensor(
                        out=rstp[:, 0:2048], in0=rstp[:, 0:2048], in1=rstp[:, 2048:4096],
                        op=OP.add), deps=[tq])
                    t_l2 = bd.emit("v", lambda p: p.tensor_tensor(
                        out=rstp[:, 0:1024], in0=rstp[:, 0:1024], in1=rstp[:, 1024:2048],
                        op=OP.add), deps=[t_l1])
                    t_l3 = bd.emit("v", lambda p: p.tensor_tensor(
                        out=rstp[:, 0:512], in0=rstp[:, 0:512], in1=rstp[:, 512:1024],
                        op=OP.add), deps=[t_l2])
                    last_tree[0] = bd.emit("v", lambda p, dit=dit: p.tensor_tensor(
                        out=yac[:, ts(dit, 512)], in0=yac[:, ts(dit, 512)],
                        in1=rstp[:, 0:512], op=OP.add), deps=[t_l3])
            bd.bar()

            if MK_STOP == "scan":
                break
            for i in range(16):
                tc = bd.emit("s", lambda p, i=i: p.activation(
                    out=gms[:, 0:512], in_=yac[:, ts(i, 512)], func=AF.Square),
                    deps=[("t", bd.cnt["t"])])
                bd.emit("t", lambda p, i=i: p.matmul(
                    psB[:, 0:512], onds[:], yac[:, ts(i, 512)],
                    start=(i == 0), stop=(i == 15)))
                bd.emit("t", lambda p, i=i: p.matmul(
                    psB[:, 512:1024], onds[:], gms[:, 0:512],
                    start=(i == 0), stop=(i == 15)), deps=[tc])
            tmm = ("t", bd.cnt["t"])
            t1 = bd.emit("v", lambda p: p.tensor_scalar_mul(sts[:, 0:512], psB[:, 0:512], 1.0 / DI), deps=[tmm])
            t2 = bd.emit("v", lambda p: p.tensor_tensor(
                out=sts[:, 512:1024], in0=sts[:, 0:512], in1=sts[:, 0:512], op=OP.mult),
                deps=[t1])
            t3 = bd.emit("v", lambda p: p.scalar_tensor_tensor(
                out=sts[:, 512:1024], in0=psB[:, 512:1024], scalar=1.0 / DI,
                in1=sts[:, 512:1024], op0=OP.mult, op1=OP.subtract), deps=[t2, tmm])
            t4 = bd.emit("s", lambda p: p.activation(
                out=sts[:, 1024:1536], in_=sts[:, 512:1024], func=AF.Ln, bias=cols[:, 204:205]),
                deps=[t3])
            t5 = bd.emit("s", lambda p: p.activation(
                out=sts[:, 512:1024], in_=sts[:, 1024:1536], func=AF.Exp, scale=-0.5),
                deps=[t4])
            t6 = bd.emit("s", lambda p: p.activation(
                out=gms[:, 0:512], in_=sts[:, 0:512], func=AF.Identity), deps=[t5])
            t7 = bd.emit("s", lambda p: p.activation(
                out=gms[:, 512:1024], in_=sts[:, 512:1024], func=AF.Identity), deps=[t5, t6])
            for i in range(16):
                ta = bd.emit("v", lambda p, i=i: p.tensor_tensor(
                    out=gbb[:, 0:512], in0=yac[:, ts(i, 512)], in1=gms[:, 0:512],
                    op=OP.subtract), deps=[t7, ("v", bd.cnt["v"])])
                tb = bd.emit("v", lambda p, i=i: p.tensor_tensor(
                    out=gbb[:, 0:512], in0=gbb[:, 0:512], in1=gms[:, 512:1024],
                    op=OP.mult), deps=[ta])
                tc2 = bd.emit("v", lambda p, i=i: p.tensor_scalar(
                    out=gbb[:, 512:1024], in0=gbb[:, 0:512], scalar1=col(C_LNG, i),
                    scalar2=col(C_LNB, i), op0=OP.mult, op1=OP.add), deps=[tb])
                bd.emit("v", lambda p, i=i: p.tensor_tensor(
                    out=yac[:, ts(i, 512)], in0=gbb[:, 512:1024], in1=zs[:, ts(i, 512)],
                    op=OP.mult), deps=[tc2])
            bd.bar()

            if MK_STOP == "iln":
                break
            prev_ld = {}
            for dmt in range(8):
                slot = wo_r[:, ts(dmt % 2, 2048)]
                tl = bd.dma("y", lambda p, dmt=dmt, slot=slot: p.dma_start(
                    out=slot, in_=bass.AP(sc["w_out"], dmt * 16 * 128 * 128,
                                          [[128, 128], [128 * 128, 16], [1, 128]])),
                    deps=[prev_ld[dmt - 2]] if dmt >= 2 else [])
                mg = []
                for kc in range(16):
                    mg.append(bd.emit("t", lambda p, kc=kc, slot=slot, dmt=dmt: p.matmul(
                        psB[:, 1024 + (dmt % 2) * 512: 1536 + (dmt % 2) * 512],
                        slot[:, ts(kc, 128)], yac[:, ts(kc, 512)],
                        start=(kc == 0), stop=(kc == 15)),
                        deps=[tl] + ([("v", bd.cnt["v"])] if dmt >= 2 else [])))
                prev_ld[dmt] = mg[-1]
                tr1 = bd.emit("v", lambda p, dmt=dmt, b=b: p.scalar_tensor_tensor(
                    out=xres[:, ts(dmt, 512)],
                    in0=psB[:, 1024 + (dmt % 2) * 512: 1536 + (dmt % 2) * 512],
                    scalar=modcol(2, b, dmt), in1=xres[:, ts(dmt, 512)],
                    op0=OP.mult, op1=OP.add), deps=[mg[-1]])
                bd.emit("v", lambda p, dmt=dmt, b=b: p.tensor_scalar_add(
                    xres[:, ts(dmt, 512)], xres[:, ts(dmt, 512)], modcol(6, b, dmt)),
                    deps=[tr1])
            bd.bar()

            if MK_STOP == "wout":
                break
            def mk_mod2(i, xhat_ap, dep):
                return bd.emit("v", lambda p, i=i, b=b: p.tensor_scalar(
                    out=h2[:, ts(i, 512)], in0=xhat_ap, scalar1=modcol(4, b, i),
                    scalar2=modcol(3, b, i), op0=OP.mult, op1=OP.add), deps=[dep])
            ln_feature_major(lambda i: xres[:, ts(i, 512)], None, 8, onds, mk_mod2, m)
            bd.bar()

            if MK_STOP == "h2":
                break
            for dc in range(8):
                bd.emit("t", lambda p, dc=dc: p.matmul(
                    psB[0:4, 0:512], gws[:, ts(dc, 4)], h2[:, ts(dc, 512)],
                    start=(dc == 0), stop=(dc == 7)))
            tmm = ("t", bd.cnt["t"])
            tlg = bd.emit("s", lambda p: p.activation(
                out=sts[0:4, 1024:1536], in_=psB[0:4, 0:512], func=AF.Identity,
                bias=thr_s[:, 1:2]), deps=[tmm])
            for mt in range(4):
                slot = e1_r[:, ts(mt % 2, 1024)]
                tl = bd.dma("y", lambda p, mt=mt, slot=slot: p.dma_start(
                    out=slot, in_=bass.AP(sc["dp_w1"], mt * 8 * 128 * 128,
                                          [[128, 128], [128 * 128, 8], [1, 128]])),
                    deps=[("t", bd.cnt["t"])] if mt >= 2 else [])
                mg = []
                for dc in range(8):
                    mg.append(bd.emit("t", lambda p, mt=mt, dc=dc, slot=slot: p.matmul(
                        psA[:, ts(mt % 2, 512)], slot[:, ts(dc, 128)], h2[:, ts(dc, 512)],
                        start=(dc == 0), stop=(dc == 7)),
                        deps=[tl] + ([("s", bd.cnt["s"])] if mt >= 2 else [])))
                bd.emit("s", lambda p, mt=mt: p.activation(
                    out=hid[:, ts(mt, 512)], in_=psA[:, ts(mt % 2, 512)],
                    func=AF.Relu, bias=col(C_DPB1, mt)), deps=[mg[-1]])
            for kc in range(4):
                bd.emit("t", lambda p, kc=kc: p.matmul(
                    psB[0:1, 512:1024], dpw2s[:, kc:kc + 1], hid[:, ts(kc, 512)],
                    start=(kc == 0), stop=(kc == 3)), deps=[("s", bd.cnt["s"])])
            tmm = ("t", bd.cnt["t"])
            te1 = bd.emit("s", lambda p: p.activation(
                out=sts[0:1, 0:512], in_=psB[0:1, 512:1024], func=AF.Exp,
                bias=dpb2s[0:1, 0:1]), deps=[tmm])
            te2 = bd.emit("s", lambda p: p.activation(
                out=gsc[0:1, 512:1024], in_=sts[0:1, 0:512], func=AF.Ln, bias=cols[0:1, 205:206]),
                deps=[te1])
            trp = bd.dma("g", lambda p: p.dma_start(
                out=gsc[0:4, 0:512],
                in_=bass.AP(gsc, 512, [[1024, 1], [0, 4], [1, 512]])), deps=[te2])
            ts1 = bd.emit("v", lambda p: p.tensor_single_scalar(
                out=gbb[0:4, 0:512], in_=gsc[0:4, 0:512], scalar=thr_s[:, 0:1],
                op=OP.is_ge), deps=[trp])
            tk1 = bd.emit("t", lambda p: p.matmul(
                psB[0:4, 1024:1536], on4s[:], gbb[0:4, 0:512], start=True, stop=True),
                deps=[ts1])
            tk2 = bd.emit("v", lambda p: p.tensor_copy(gsc[0:4, 512:1024], psB[0:4, 1024:1536]),
                          deps=[tk1])
            tlb = bd.emit("s", lambda p: p.activation(
                out=gbb[0:4, 512:1024], in_=sts[0:4, 1024:1536], func=AF.Identity), deps=[tlg])
            td = bd.emit("t", lambda p: p.matmul(
                psA[0:16, 0:512], pcmps[:], gbb[0:4, 512:1024], start=True, stop=True),
                deps=[tlb, ("s", bd.cnt["s"]), ("v", bd.cnt["v"])])
            tgt = bd.emit("v", lambda p: p.tensor_single_scalar(
                out=gbb[0:16, 512:1024], in_=psA[0:16, 0:512], scalar=0.0,
                op=OP.is_lt), deps=[td])
            teq = bd.emit("v", lambda p: p.tensor_scalar(
                out=gsc[0:16, 0:512], in0=psA[0:16, 0:512], scalar1=0.0,
                scalar2=ltms[:, 0:1], op0=OP.is_equal, op1=OP.mult), deps=[td])
            tct = bd.emit("v", lambda p: p.tensor_tensor(
                out=gbb[0:16, 512:1024], in0=gbb[0:16, 512:1024],
                in1=gsc[0:16, 0:512], op=OP.add), deps=[tgt, teq])
            trk = bd.emit("t", lambda p: p.matmul(
                psA[0:4, 512:1024], prnks[:], gbb[0:16, 512:1024], start=True, stop=True),
                deps=[tct, ("s", bd.cnt["s"])])
            tin = bd.emit("v", lambda p: p.tensor_tensor(
                out=gsc[0:4, 0:512], in0=psA[0:4, 512:1024], in1=gsc[0:4, 512:1024],
                op=OP.is_lt), deps=[trk, tk2])
            tex = bd.emit("s", lambda p: p.activation(
                out=sts[0:4, 1024:1536], in_=sts[0:4, 1024:1536], func=AF.Exp), deps=[tlb, tin])
            twt = bd.emit("v", lambda p: p.tensor_tensor(
                out=gsc[0:4, 512:1024], in0=sts[0:4, 1024:1536], in1=gsc[0:4, 0:512], op=OP.mult),
                deps=[tex])
            twb = bd.emit("s", lambda p: p.activation(
                out=gbb[0:4, 0:512], in_=gsc[0:4, 512:1024], func=AF.Identity), deps=[twt])
            tz = bd.emit("t", lambda p: p.matmul(
                psB[0:4, 1024:1536], on4s[:], gbb[0:4, 0:512], start=True, stop=True),
                deps=[twb])
            trz = bd.emit("v", lambda p: p.reciprocal(gsc[0:4, 0:512], psB[0:4, 1024:1536]),
                          deps=[tz])
            tw = bd.emit("v", lambda p: p.tensor_tensor(
                out=gsc[0:4, 512:1024], in0=gsc[0:4, 512:1024], in1=gsc[0:4, 0:512],
                op=OP.mult), deps=[trz])
            twc = bd.emit("s", lambda p: p.activation(
                out=wrw[:], in_=gsc[0:4, 512:1024], func=AF.Identity), deps=[tw])
            for e in range(4):
                bd.dma("g", lambda p, e=e: p.dma_start(
                    out=wbc[:, ts(e, 512)],
                    in_=bass.AP(wrw, e * 512, [[512, 1], [0, 128], [1, 512]])),
                    deps=[twc])
            bd.bar()

            if MK_STOP == "gate":
                break
            prev_e1 = {}
            prev_e2 = {}
            for e in range(NE):
                for ht in range(16):
                    gi = e * 16 + ht
                    slot = e1_r[:, ts(gi % 2, 1024)]
                    tl = bd.dma("y", lambda p, e=e, ht=ht, slot=slot: p.dma_start(
                        out=slot, in_=bass.AP(sc["ew1"], (e * 16 + ht) * 8 * 128 * 128,
                                              [[128, 128], [128 * 128, 8], [1, 128]])),
                        deps=[prev_e1[gi - 2]] if gi >= 2 else [])
                    mg = []
                    for dc in range(8):
                        mg.append(bd.emit("t", lambda p, dc=dc, slot=slot, ht=ht: p.matmul(
                            psA[:, ts(ht % 2, 512)], slot[:, ts(dc, 128)],
                            h2[:, ts(dc, 512)], start=(dc == 0), stop=(dc == 7)),
                            deps=[tl] + ([("s", bd.cnt["s"])] if gi >= 2 else [])))
                    prev_e1[gi] = mg[-1]
                    bd.emit("s", lambda p, e=e, ht=ht: p.activation(
                        out=hid[:, ts(ht, 512)], in_=psA[:, ts(ht % 2, 512)],
                        func=AF.Relu, bias=col(C_EB1, e * 16 + ht)), deps=[mg[-1]])
                shid = ("s", bd.cnt["s"])
                for dmt in range(8):
                    gi = e * 8 + dmt
                    slot = wo_r[:, ts(gi % 2, 2048)]
                    tl = bd.dma("y", lambda p, e=e, dmt=dmt, slot=slot: p.dma_start(
                        out=slot, in_=bass.AP(sc["ew2"], (e * 8 + dmt) * 16 * 128 * 128,
                                              [[128, 128], [128 * 128, 16], [1, 128]])),
                        deps=[prev_e2[gi - 2]] if gi >= 2 else [])
                    mg = []
                    for kc in range(16):
                        mg.append(bd.emit("t", lambda p, kc=kc, slot=slot, dmt=dmt: p.matmul(
                            psB[:, 1024 + (dmt % 2) * 512: 1536 + (dmt % 2) * 512],
                            slot[:, ts(kc, 128)], hid[:, ts(kc, 512)],
                            start=(kc == 0), stop=(kc == 15)),
                            deps=[tl, shid] + ([("v", bd.cnt["v"])] if gi >= 2 else [])))
                    prev_e2[gi] = mg[-1]
                    if e == 0:
                        bd.emit("v", lambda p, e=e, dmt=dmt: p.scalar_tensor_tensor(
                            out=rstp[:, ts(dmt, 512)],
                            in0=psB[:, 1024 + (dmt % 2) * 512: 1536 + (dmt % 2) * 512],
                            scalar=col(C_EB2, e * 8 + dmt), in1=wbc[:, ts(e, 512)],
                            op0=OP.add, op1=OP.mult), deps=[mg[-1]])
                    else:
                        tc1 = bd.emit("v", lambda p, e=e, dmt=dmt: p.scalar_tensor_tensor(
                            out=gbb[:, 0:512],
                            in0=psB[:, 1024 + (dmt % 2) * 512: 1536 + (dmt % 2) * 512],
                            scalar=col(C_EB2, e * 8 + dmt), in1=wbc[:, ts(e, 512)],
                            op0=OP.add, op1=OP.mult), deps=[mg[-1]])
                        bd.emit("v", lambda p, dmt=dmt: p.tensor_tensor(
                            out=rstp[:, ts(dmt, 512)], in0=rstp[:, ts(dmt, 512)],
                            in1=gbb[:, 0:512], op=OP.add), deps=[tc1])
            bd.bar()

            if MK_STOP == "ffn":
                break
            for dmt in range(8):
                bd.emit("v", lambda p, dmt=dmt, b=b: p.scalar_tensor_tensor(
                    out=xres[:, ts(dmt, 512)], in0=rstp[:, ts(dmt, 512)],
                    scalar=modcol(5, b, dmt), in1=xres[:, ts(dmt, 512)],
                    op0=OP.mult, op1=OP.add))
            bd.bar()
            for q in range(4):
                tts = []
                for dmt in range(8):
                    tts.append(bd.emit("t", lambda p, q=q, dmt=dmt: p.transpose(
                        psA[:, (q % 2) * 1024 + dmt * 128:(q % 2) * 1024 + dmt * 128 + 128],
                        xres[:, dmt * 512 + q * 128: dmt * 512 + q * 128 + 128], idn[:]),
                        deps=[("s", bd.cnt["s"])] if q >= 2 else []))
                tev = bd.emit("s", lambda p, q=q: p.activation(
                    out=stg_o[:, 0:1024],
                    in_=psA[:, (q % 2) * 1024:(q % 2) * 1024 + 1024], func=AF.Identity),
                    deps=tts + ([("dy", bd.cnt["dy"])] if q >= 1 else []))
                bd.dma("y", lambda p, q=q, b=b, c=c: p.dma_start(
                    out=bass.AP(out_t, (b * T + c * TAU + q * 128) * D, [[D, 128], [1, D]]),
                    in_=stg_o[:, 0:1024]), deps=[tev])

        bd.flush()
        bd.bar()
        if MK_DBG:
            _dt = {t.name: t for t in [stg_i, stg_o, xres, h1, xin, zs, dtb, yac,
                   abar, rstp, hstp, brep, crep, stat, h2, hid, sts, gms, lnc,
                   gsc, gbb, wbc, wrw, dtr_s, bcs, wxs, wdts, a_sb, cols, mods,
                   modp, cTs]}[MK_DBG]
            _nf = _dt.shape[1]
            bd.dma("g", lambda p: p.dma_start(
                out=dbg_t.ap()[0:_dt.shape[0], 0:_nf], in_=_dt[:]))
            bd.bar()
        final_dy = bd.cnt["dy"]
        final_dg = bd.cnt["dg"]
        bd.raw("y", lambda p: (p.wait_ge(bd.sem["dy"], final_dy),
                               p.wait_ge(bd.sem["dg"], final_dg)))
        bd.flush()

    return nc


_CACHE = {}


def _host_prep(inputs):
    f = np.float32
    W_in = np.ascontiguousarray(inputs["L0_attn_W_in"], f)
    w_in = np.ascontiguousarray(
        W_in.reshape(8, 128, 32, 128).transpose(2, 0, 1, 3))
    w_x = np.ascontiguousarray(
        inputs["L0_attn_W_x"].reshape(16, 128, 96).transpose(1, 0, 2).reshape(128, 1536))
    w_dt = np.ascontiguousarray(inputs["L0_attn_W_dt"])
    w_out = np.ascontiguousarray(
        inputs["L0_attn_W_out"].reshape(16, 128, 8, 128).transpose(2, 0, 1, 3))
    ew1 = np.ascontiguousarray(
        inputs["L0_ffn_ew1"].reshape(NE, 8, 128, 16, 128).transpose(0, 3, 1, 2, 4))
    ew2 = np.ascontiguousarray(
        inputs["L0_ffn_ew2"].reshape(NE, 16, 128, 8, 128).transpose(0, 3, 1, 2, 4))
    wmod = np.ascontiguousarray(
        inputs["L0_wmod"].reshape(8, 128, 48, 128).transpose(2, 0, 1, 3))
    dp_w1 = np.ascontiguousarray(
        inputs["L0_ffn_dp_w1"].reshape(8, 128, 4, 128).transpose(2, 0, 1, 3))
    dp_w2 = np.ascontiguousarray(inputs["L0_ffn_dp_w2"].reshape(4, 128).T)
    gate_w = np.ascontiguousarray(
        inputs["L0_ffn_gate_w"].reshape(8, 128, 4).transpose(1, 0, 2).reshape(128, 32))
    a_log = np.ascontiguousarray(
        inputs["L0_attn_A_log"].reshape(16, 128, 16).transpose(1, 0, 2).reshape(128, 256))

    colpack = np.zeros((128, 208), f)
    colpack[:, 0:32] = inputs["L0_attn_b_in"].reshape(32, 128).T
    colpack[:, 32:48] = inputs["L0_attn_b_dt"].reshape(16, 128).T
    colpack[:, 48:64] = inputs["L0_attn_D"].reshape(16, 128).T
    colpack[:, 64:80] = inputs["L0_attn_ln_g"].reshape(16, 128).T
    colpack[:, 80:96] = inputs["L0_attn_ln_b"].reshape(16, 128).T
    colpack[:, 96:104] = inputs["L0_attn_b_out"].reshape(8, 128).T
    colpack[:, 104:168] = inputs["L0_ffn_eb1"].reshape(4 * 16, 128).T
    colpack[:, 168:200] = inputs["L0_ffn_eb2"].reshape(4 * 8, 128).T
    colpack[:, 200:204] = inputs["L0_ffn_dp_b1"].reshape(4, 128).T
    colpack[:, 204] = EPS
    colpack[:, 205] = 1.0

    thr = np.zeros((4, 2), f)
    thr[:, 0] = [-1e30, 0.75, 1.25, 1.75]
    thr[:, 1] = inputs["L0_ffn_gate_b"]
    ltm = np.zeros((16, 1), f)
    pcmp = np.zeros((4, 16), f)
    for e in range(4):
        for e2 in range(4):
            pcmp[e, 4 * e + e2] += 1.0
            pcmp[e2, 4 * e + e2] -= 1.0
            if e2 < e:
                ltm[4 * e + e2, 0] = 1.0
    prnk = np.zeros((16, 4), f)
    for e in range(4):
        prnk[4 * e: 4 * e + 4, e] = 1.0
    on4s = np.ones((4, 4), f)
    ident = np.eye(128, dtype=f)
    onds = np.ones((128, 128), f)
    dpb2 = inputs["L0_ffn_dp_b2"].reshape(1, 1).astype(f)

    shared = dict(w_in=w_in, w_x=w_x, w_dt=w_dt, w_out=w_out, ew1=ew1, ew2=ew2,
                  wmod=wmod, dp_w1=dp_w1, dp_w2=dp_w2, gate_w=gate_w,
                  a_log=a_log, colpack=colpack, thrs=thr, ltm=ltm, pcmp=pcmp,
                  prnk=prnk, ones4=on4s, ident=ident, onesd=onds, dpb2=dpb2)

    x = np.ascontiguousarray(inputs["L0_attn_x"], f)
    c_all = np.ascontiguousarray(inputs["L0_c"], f)
    in_maps = []
    for core in range(8):
        im = dict(shared)
        im["x"] = np.ascontiguousarray(x[2 * core: 2 * core + 2])
        cc = c_all[2 * core: 2 * core + 2]
        im["cT"] = np.ascontiguousarray(cc.T.reshape(8, 128, 2).transpose(1, 0, 2).reshape(128, 16))
        in_maps.append(im)
    return in_maps


def kernel(**inputs):
    in_maps = _host_prep(inputs)
    if "nc" not in _CACHE:
        _CACHE["nc"] = build_nc()
    nc = _CACHE["nc"]
    res = run_bass_kernel_spmd(nc, in_maps, core_ids=list(range(8)))
    out = np.empty((16, T, D), np.float32)
    for core in range(8):
        out[2 * core: 2 * core + 2] = res.results[core]["out"]
    return out
